# version 01
# baseline (speedup 1.0000x reference)
import contextlib
import numpy as np
import ml_dtypes
import concourse.bass as bass
import concourse.mybir as mybir
from concourse.bass_utils import run_bass_kernel_spmd

F32 = mybir.dt.float32
BF16 = mybir.dt.bfloat16
AF = mybir.ActivationFunctionType
ALU = mybir.AluOpType
AX = mybir.AxisListType

NT = 32
L = 128
DM = 1024
H = 4
EPS_LN = 1e-5
EPS_PU = 1e-6
EPS_DEN = 1e-6
MAGIC = 1234.5

KDEBUG = False
RACE_DETECT = False
DEV_OWN_SLOT = False
TRUNC = 4
AXL = 6
BXL = 6


_COUNTING = [False]


class Ctr:
    def __init__(self, handle):
        self.h = handle
        self.n = 0
        self.marks = {}

    def inc(self, instr, k=1, mark=None):
        if getattr(self, "_chained_last", None) is not instr:
            instr.then_inc(self.h, k)
            self.n += k
        if mark is not None:
            if _COUNTING[0]:
                assert mark not in self.marks, mark
                self.marks[mark] = self.n
            else:
                assert self.marks.get(mark) == self.n, (mark, self.marks.get(mark), self.n)
        return self.n

    def mark_only(self, mark):
        if _COUNTING[0]:
            assert mark not in self.marks, mark
            self.marks[mark] = self.n
        else:
            assert self.marks.get(mark) == self.n, (mark, self.marks.get(mark), self.n)

    def wait(self, eng, mark):
        if mark is None:
            return
        if _COUNTING[0]:
            return
        eng.wait_ge(self.h, self.marks[mark])


class _FakeInstr:
    def then_inc(self, *a, **k):
        return self


class _FakeEng:
    def switch_core_id(self, n):
        return [_FakeArm(i) for i in range(n)]

    def __getattr__(self, name):
        def fn(*a, **k):
            return _FakeInstr()

        return fn


class _FakeArm(int):
    pass


CHAIN_OPS = {
    "activation", "tensor_tensor", "tensor_scalar", "tensor_scalar_mul",
    "tensor_scalar_add", "tensor_scalar_min", "tensor_scalar_max", "tensor_copy",
    "bn_stats", "bn_aggr", "reduce_max", "reduce_sum", "tensor_reduce",
    "reciprocal", "scalar_tensor_tensor", "memset", "tensor_add", "tensor_sub",
    "tensor_mul",
}


class CEng:

    def __init__(self, eng, ctr):
        self._e = eng
        self._c = ctr
        self._first = True

    def __getattr__(self, nm):
        fn = getattr(self._e, nm)
        if nm in CHAIN_OPS:
            def wrapped(*a, **k):
                if not self._first:
                    self._e.wait_ge(self._c.h, self._c.n)
                self._first = False
                ins = fn(*a, **k)
                ins.then_inc(self._c.h, 1)
                self._c.n += 1
                self._c._chained_last = ins
                return ins

            return wrapped
        return fn


def build_nc():
    nc = bass.Bass("TRN2", target_bir_lowering=False, debug=False, num_devices=8, detect_race_conditions=RACE_DETECT)

    x_d = nc.dram_tensor("x", [NT * L, DM], F32, kind="ExternalInput").ap()
    wq_d = nc.dram_tensor("Wq", [256, 1024], F32, kind="ExternalInput").ap()
    wk_d = nc.dram_tensor("Wk", [256, 1024], F32, kind="ExternalInput").ap()
    wv_d = nc.dram_tensor("Wv", [256, 1024], F32, kind="ExternalInput").ap()
    wo_d = nc.dram_tensor("Wo", [1024, 256], F32, kind="ExternalInput").ap()
    wf_d = nc.dram_tensor("Wff", [1024, 1024], F32, kind="ExternalInput").ap()
    gw_d = nc.dram_tensor("gw", [8, 1024], F32, kind="ExternalInput").ap()
    u2_d = nc.dram_tensor("U2", [128, 1024], F32, kind="ExternalInput").ap()
    lv_d = nc.dram_tensor("loraV", [8, 1024, 16], F32, kind="ExternalInput").ap()
    idn_d = nc.dram_tensor("idn", [128, 128], BF16, kind="ExternalInput").ap()
    msk_d = nc.dram_tensor("maskUT", [128, 128], F32, kind="ExternalInput").ap()
    role_d = nc.dram_tensor("role", [128, 1], F32, kind="ExternalInput").ap()
    out_d = nc.dram_tensor("out", [NT * L, DM], F32, kind="ExternalOutput").ap()

    part_d = nc.dram_tensor("partial", [NT * L, DM], F32)
    shS = nc.dram_tensor("shS", [2 * 64, 260], F32, addr_space="Shared")
    shF = nc.dram_tensor("shF", [1, 8], F32, addr_space="Shared")

    dbg = {}
    if KDEBUG:
        for nm, shp in [
            ("dbg_S", [64, 260]), ("dbg_sprev", [64, 260]),
            ("dbg_xn", [128, DM]), ("dbg_ktm", [128, 256]),
            ("dbg_g", [128, 8]), ("dbg_u", [128, 128]),
        ]:
            dbg[nm] = nc.dram_tensor(nm, shp, F32, kind="ExternalOutput").ap()
        dbg["dbg_pref"] = nc.dram_tensor("dbg_pref", [64, (NT + 1) * 260], BF16, kind="ExternalOutput").ap()
        dbg["dbg_kfm"] = nc.dram_tensor("dbg_kfm", [64, 4 * NT * L], BF16, kind="ExternalOutput").ap()
        dbg["dbg_vaug"] = nc.dram_tensor("dbg_vaug", [128, NT * 260], BF16, kind="ExternalOutput").ap()

    es = contextlib.ExitStack()
    E = es.enter_context
    with es:
        xbuf = [E(nc.sbuf_tensor(f"xbuf{i}", [128, DM], F32)) for i in range(2)]
        pbuf = [E(nc.sbuf_tensor(f"pbuf{i}", [128, DM], F32)) for i in range(2)]
        xnfm = E(nc.sbuf_tensor("xnfm", [128, 8 * NT * L], BF16))
        kfm = E(nc.sbuf_tensor("kfm", [64, 4 * NT * L], BF16))
        vaug = E(nc.sbuf_tensor("vaug", [128, NT * 260], BF16))
        prefix = E(nc.sbuf_tensor("prefix", [64, (NT + 1) * 260], BF16))
        s_run = E(nc.sbuf_tensor("s_run", [64, 260], F32))
        s_prev = E(nc.sbuf_tensor("s_prev", [64, 260], BF16))
        s_pk = [E(nc.sbuf_tensor(f"s_pk{i}", [64, 260], BF16)) for i in range(2)]
        gath = E(nc.sbuf_tensor("gath", [64, 260], F32))
        wkv = E(nc.sbuf_tensor("wkv", [128, 8 * 512], BF16))
        wgu = E(nc.sbuf_tensor("wgu", [128, 8 * 136], BF16))
        wqr = E(nc.sbuf_tensor("wqr", [128, 8 * 256], BF16))
        wff = E(nc.sbuf_tensor("wff", [128, 8 * 1024], BF16))
        wot = E(nc.sbuf_tensor("wot", [128, 2 * 1024], BF16))
        vfl = E(nc.sbuf_tensor("vfl", [128, 1024], BF16))
        stage = [E(nc.sbuf_tensor(f"stage{i}", [128, 1024], BF16)) for i in range(2)]
        eA = [E(nc.sbuf_tensor(f"eA{i}", [128, 256], F32)) for i in range(2)]
        eB = [E(nc.sbuf_tensor(f"eB{i}", [128, 256], F32)) for i in range(2)]
        ktm = [E(nc.sbuf_tensor(f"ktm{i}", [128, 256], BF16)) for i in range(2)]
        qfm = [E(nc.sbuf_tensor(f"qfm{i}", [64, 512], BF16)) for i in range(2)]
        ams = [E(nc.sbuf_tensor(f"ams{i}", [128, 512], BF16)) for i in range(2)]
        ytm = [E(nc.sbuf_tensor(f"ytm{i}", [128, 256], BF16)) for i in range(2)]
        yfm = [E(nc.sbuf_tensor(f"yfm{i}", [128, 256], BF16)) for i in range(2)]
        ubf = [E(nc.sbuf_tensor(f"ubf{i}", [128, 128], BF16)) for i in range(2)]
        uT = [E(nc.sbuf_tensor(f"uT{i}", [128, 128], BF16)) for i in range(2)]
        gsc = [E(nc.sbuf_tensor(f"gsc{i}", [128, 8], F32)) for i in range(2)]
        st12 = [E(nc.sbuf_tensor(f"st12{i}", [128, 12], F32)) for i in range(2)]
        mvb = [E(nc.sbuf_tensor(f"mv{i}", [128, 8], F32)) for i in range(2)]
        nrm = [E(nc.sbuf_tensor(f"nrm{i}", [128, 8], F32)) for i in range(2)]
        dss = [E(nc.sbuf_tensor(f"dss{i}", [128, 8], F32)) for i in range(2)]
        ident = E(nc.sbuf_tensor("ident", [128, 128], BF16))
        mskt = E(nc.sbuf_tensor("mskt", [128, 128], F32))
        rolet = E(nc.sbuf_tensor("rolet", [128, 1], F32))
        eps6 = E(nc.sbuf_tensor("eps6", [128, 1], F32))
        eps5 = E(nc.sbuf_tensor("eps5", [128, 1], F32))
        flagsrc = E(nc.sbuf_tensor("flagsrc", [1, 8], F32))
        pA = [E(nc.psum_tensor(f"pA{i}", [128, 512], F32)) for i in range(2)]
        pO = [E(nc.psum_tensor(f"pO{i}", [128, 512], F32)) for i in range(2)]
        pM = [E(nc.psum_tensor(f"pM{i}", [128, 512], F32)) for i in range(2)]
        pT = [E(nc.psum_tensor(f"pT{i}", [128, 1024], BF16)) for i in range(2)]

        sPE = Ctr(E(nc.semaphore("sPE")))
        sDV = Ctr(E(nc.semaphore("sDV")))
        sAC = Ctr(E(nc.semaphore("sAC")))
        sPL = Ctr(E(nc.semaphore("sPL")))
        dX = [Ctr(E(nc.semaphore(f"dX{i}"))) for i in range(2)]
        dW = [Ctr(E(nc.semaphore(f"dW{i}"))) for i in range(2)]
        dPS = [Ctr(E(nc.semaphore(f"dPS{i}"))) for i in range(2)]
        dPL = [Ctr(E(nc.semaphore(f"dPL{i}"))) for i in range(2)]
        dO = [Ctr(E(nc.semaphore(f"dO{i}"))) for i in range(2)]
        dC = Ctr(E(nc.semaphore("dC")))
        dS = Ctr(E(nc.semaphore("dS")))

        block = E(nc.Block())

        def m(name, t):
            return f"{name}.{t}" if t >= 0 else None

        def xnfm_v(c, t):
            return xnfm[:, c * NT * L + t * L : c * NT * L + (t + 1) * L]

        def xnfm_3d(t):
            return xnfm[:].rearrange("p (c n) -> p c n", c=8)[:, :, t * L : (t + 1) * L]

        def kfm_v(h, t):
            return kfm[0:64, h * NT * L + t * L : h * NT * L + (t + 1) * L]

        def kfm_3d(t):
            return kfm[0:64, :].rearrange("p (c n) -> p c n", c=4)[:, :, t * L : (t + 1) * L]

        def vaug_v(t, h):
            return vaug[:, t * 260 + h * 65 : t * 260 + (h + 1) * 65]

        def vaug_hd(t):
            return vaug[:, t * 260 : (t + 1) * 260].rearrange("p (h c) -> p h c", h=4)

        def pref_v(t):
            return prefix[0:64, t * 260 : (t + 1) * 260]

        WBLK = []
        wkv3 = wkv[:].rearrange("p (c n) -> p c n", c=8)
        wqr3 = wqr[:].rearrange("p (c n) -> p c n", c=8)
        wff3 = wff[:].rearrange("p (c n) -> p c n", c=8)
        wgu3 = wgu[:].rearrange("p (c n) -> p c n", c=8)
        wot3 = wot[:].rearrange("p (c n) -> p c n", c=2)
        vfl3 = vfl[:].rearrange("p (c n) -> p c n", c=1)
        for rb in range(2):
            WBLK.append((wk_d[rb * 128 : (rb + 1) * 128, :], wkv3[:, :, rb * 128 : rb * 128 + 128], "n"))
        for rb in range(2):
            WBLK.append((wv_d[rb * 128 : (rb + 1) * 128, :], wkv3[:, :, 256 + rb * 128 : 256 + rb * 128 + 128], "n"))
        for rb in range(2):
            WBLK.append((wq_d[rb * 128 : (rb + 1) * 128, :], wqr3[:, :, rb * 128 : rb * 128 + 128], "n"))
        for rb in range(8):
            WBLK.append((wf_d[rb * 128 : (rb + 1) * 128, :], wff3[:, :, rb * 128 : rb * 128 + 128], "n"))
        WBLK.append((gw_d[:, :], wgu3[:, :, 0:8], "g"))
        WBLK.append((u2_d[:, :], wgu3[:, :, 8:136], "n"))
        for rb in range(8):
            WBLK.append((wo_d[rb * 128 : (rb + 1) * 128, :], wot3[:, :, rb * 128 : rb * 128 + 128], "n"))
        lv4 = lv_d.rearrange("e (oc o) r -> oc o e r", o=128)
        for oc in range(8):
            WBLK.append((lv4[oc], vfl3[:, :, oc * 128 : oc * 128 + 128], "v"))
        NB = len(WBLK)

        def nchunks(i):
            src, _, kind = WBLK[i]
            if kind == "g":
                return 8
            if kind == "v":
                return 1
            return src.shape[1] // 128

        def emit_sync(y):
            y.dma_start(out=ident[:], in_=idn_d).then_inc(dC.h, 16)
            y.dma_start(out=mskt[:], in_=msk_d).then_inc(dC.h, 16)
            y.dma_start(out=rolet[:], in_=role_d).then_inc(dC.h, 16)
            dC.n += 48
            dC.mark_only("consts")
            if TRUNC >= 1:
                for t in range(NT):
                    if t >= 2:
                        sDV.wait(y, m("a_var", t - 2))
                        if AXL >= 2:
                            sPL.wait(y, m("a_xn", t - 2))
                    dX[t % 2].inc(y.dma_start(out=xbuf[t % 2][:], in_=x_d[t * L : (t + 1) * L, :]), 16, m("ax", t))
            if TRUNC < 3:
                if TRUNC >= 1:
                    sDV.wait(y, "endA_dv")
                    if AXL >= 5:
                        sAC.wait(y, m("a_kfev", NT - 1))
                    if AXL >= 4:
                        sDV.wait(y, m("a_p", NT - 1))
                else:
                    sAC.wait(y, f"wev{NB - 1}")
                if TRUNC >= 2:
                    sDV.wait(y, "sprev")
                dO[0].inc(y.dma_start(out=out_d[0:128, :], in_=pbuf[0][:]), 16)
                y.wait_ge(dO[0].h, dO[0].n)
                return
            sDV.wait(y, "endA_dv")
            for t in range(NT):
                if t >= 2:
                    sDV.wait(y, m("b1_part", t - 2))
                else:
                    sPL.wait(y, m("a_xn", NT - 2 + t))
                dX[t % 2].inc(y.dma_start(out=xbuf[t % 2][:], in_=x_d[t * L : (t + 1) * L, :]), 16, m("b1x", t))
                sDV.wait(y, m("b1_part", t))
                dPS[t % 2].inc(y.dma_start(out=part_d.ap()[t * L : (t + 1) * L, :], in_=pbuf[t % 2][:]), 16, m("b1st", t))
            if TRUNC < 4:
                sDV.wait(y, "endB1_dv")
                dPS[(NT - 1) % 2].wait(y, m("b1st", NT - 1))
                dPS[(NT - 2) % 2].wait(y, m("b1st", NT - 2))
                dO[0].inc(y.dma_start(out=out_d[0:128, :], in_=pbuf[0][:]), 16)
                y.wait_ge(dO[0].h, dO[0].n)
                return
            sDV.wait(y, "endB1_dv")
            if BXL < 6:
                ends = {1: ("sAC", "b2_qfev"), 2: ("sPE", "b2_AT"), 3: ("sDV", "b2_am"), 4: ("sPE", "b2_num"), 5: ("sAC", "b2_yfev")}
                sem, mk = ends[BXL]
                {"sAC": sAC, "sPE": sPE, "sDV": sDV}[sem].wait(y, m(mk, NT - 1))
                dO[0].inc(y.dma_start(out=out_d[0:128, :], in_=pbuf[0][:]), 16)
                y.wait_ge(dO[0].h, dO[0].n)
                return
            for t in range(NT):
                if t >= 2:
                    sDV.wait(y, m("b2_out", t - 2))
                dPS[t % 2].wait(y, m("b1st", t))
                dPL[t % 2].inc(y.dma_start(out=xbuf[t % 2][:], in_=part_d.ap()[t * L : (t + 1) * L, :]), 16, m("b2pl", t))
                sDV.wait(y, m("b2_out", t))
                dO[t % 2].inc(y.dma_start(out=out_d[t * L : (t + 1) * L, :], in_=pbuf[t % 2][:]), 16, m("ost", t))
            if KDEBUG:
                sDV.wait(y, "endB2_dv")
                dO[0].inc(y.dma_start(out=dbg["dbg_S"], in_=s_run[0:64, :]), 16)
                dO[1].inc(y.dma_start(out=dbg["dbg_sprev"], in_=gath[0:64, :]), 16)
                dO[0].inc(y.dma_start(out=dbg["dbg_pref"], in_=prefix[0:64, :]), 16)
                dO[0].inc(y.dma_start(out=dbg["dbg_kfm"], in_=kfm[0:64, :]), 16)
                dO[0].inc(y.dma_start(out=dbg["dbg_vaug"], in_=vaug[:, :]), 16)
            y.wait_ge(dO[0].h, dO[0].n)
            y.wait_ge(dO[1].h, dO[1].n)

        def emit_gpsimd(g):
            g = CEng(g, sPL)
            g.memset(eps6[:], EPS_DEN)
            g.memset(eps5[:], EPS_LN)
            g.memset(flagsrc[:], MAGIC)
            g.memset(s_run[0:64, :], 0.0)
            g.memset(pref_v(0), 0.0)
            for t_ in range(NT):
                for h in range(H):
                    li_ms = g.memset(vaug[:, t_ * 260 + h * 65 + 64 : t_ * 260 + h * 65 + 65], 1.0)
            sPL.inc(li_ms, 1, "setup_const")
            for i in range(NB):
                if i >= 2:
                    sPE.wait(g, f"wT{i - 2}")
                kind = WBLK[i][2]
                if kind == "v":
                    dst = stage[i % 2][:, 0:128].rearrange("p (e r) -> p e r", e=8)
                else:
                    src = WBLK[i][0]
                    dst = stage[i % 2][0 : src.shape[0], 0 : src.shape[1]]
                dW[i % 2].inc(g.dma_start(out=dst, in_=WBLK[i][0]), 16, f"wld{i}")
            if TRUNC < 1:
                return
            sPE.wait(g, f"wT{NB - 1}")
            for t in range(NT):
                if AXL >= 2:
                    dX[t % 2].wait(g, m("ax", t))
                    sAC.wait(g, m("a_rstd", t))
                    if t >= 2:
                        sPE.wait(g, m("a_xnT", t - 2))
                    sPL.inc(
                        g.tensor_scalar(
                            stage[t % 2][:, 0:1024],
                            xbuf[t % 2][:],
                            mvb[t % 2][:, 0:1],
                            mvb[t % 2][:, 1:2],
                            ALU.subtract,
                            ALU.mult,
                        ),
                        1,
                        m("a_xn", t),
                    )
                pass
            if TRUNC < 2:
                return
            sDV.wait(g, m("a_S", NT - 1))
            for arm in g.switch_core_id(128):
                par = int(arm) % 2
                g.dma_start(out=shS.ap()[par * 64 : par * 64 + 64, :], in_=s_run[0:64, :]).then_inc(dS.h, 16)
            dS.n += 16
            g.wait_ge(dS.h, dS.n)
            for arm in g.switch_core_id(128):
                par = int(arm) % 2
                g.dma_start(out=shF.ap()[0:1, par * 4 : par * 4 + 4], in_=flagsrc[0:1, 0:4]).then_inc(dS.h, 16)
            dS.n += 16
            if TRUNC >= 3:
                sDV.wait(g, m("b1_part", max(0, NT - 8)))
            for arm in g.switch_core_id(128):
                par = int(arm) % 2
                rdpar = par if DEV_OWN_SLOT else (1 - par)
                g.dma_start(out=gath[0:64, :], in_=shS.ap()[rdpar * 64 : rdpar * 64 + 64, :]).then_inc(dS.h, 16)
            dS.n += 16
            dS.mark_only("gath")
            if TRUNC < 4:
                return
            pass

        def emit_tensor(t):
            for i in range(NB):
                dW[i % 2].wait(t, f"wld{i}")
                if i == 0:
                    dC.wait(t, "consts")
                if i >= 2:
                    sAC.wait(t, f"wev{i - 2}")
                if WBLK[i][2] == "g":
                    for c in range(8):
                        li = t.transpose(
                            pT[i % 2][:, c * 128 : c * 128 + 8],
                            stage[i % 2][0:8, c * 128 : c * 128 + 128],
                            ident[0:8, 0:8],
                        )
                else:
                    for c in range(nchunks(i)):
                        li = t.transpose(
                            pT[i % 2][:, c * 128 : c * 128 + 128],
                            stage[i % 2][:, c * 128 : c * 128 + 128],
                            ident[:],
                        )
                sPE.inc(li, 1, f"wT{i}")
            if TRUNC < 1:
                return
            sPL.wait(t, "setup_const")
            for tt in range(NT):
                if AXL >= 2:
                    sPL.wait(t, m("a_xn", tt))
                    if tt >= 2:
                        sAC.wait(t, m("a_xnev", tt - 2))
                    if AXL >= 5:
                        sAC.wait(t, m("a_kfev", tt - 1))
                    xn_tm = stage[tt % 2]
                    for c in range(8):
                        li = t.transpose(pT[tt % 2][:, c * 128 : c * 128 + 128], xn_tm[:, c * 128 : c * 128 + 128], ident[:])
                    sPE.inc(li, 1, m("a_xnT", tt))
                if AXL >= 3:
                    sAC.wait(t, m("a_xnev", tt))
                    if tt >= 2:
                        if AXL >= 4:
                            sAC.wait(t, m("a_vev", tt - 2))
                            sAC.wait(t, m("a_kelu", tt - 2))
                            sDV.wait(t, m("a_p", tt - 2))
                    for c in range(8):
                        li = t.matmul(
                            pA[tt % 2][:],
                            xnfm_v(c, tt),
                            wkv[:, c * 512 : (c + 1) * 512],
                            start=(c == 0),
                            stop=(c == 7),
                        )
                    sPE.inc(li, 1, m("a_kv", tt))
                if AXL >= 5:
                    sDV.wait(t, m("a_ktm", tt))
                    sAC.wait(t, m("a_xnev", tt - 1))
                    if tt >= 2:
                        sAC.wait(t, m("a_kfev", tt - 2))
                    for h in range(H):
                        li = t.transpose(
                            pT[(tt + 1) % 2][0:64, h * 128 : (h + 1) * 128],
                            ktm[tt % 2][:, h * 64 : (h + 1) * 64],
                            ident[:],
                        )
                    sPE.inc(li, 1, m("a_kT", tt))
                if AXL >= 6:
                    sAC.wait(t, m("a_vev", tt))
                    if tt >= 2:
                        sDV.wait(t, m("a_S", tt - 2))
                    for h in range(H):
                        li = t.matmul(
                            pM[tt % 2][0:64, h * 65 : (h + 1) * 65],
                            ktm[tt % 2][:, h * 64 : (h + 1) * 64],
                            vaug_v(tt, h),
                            start=(h == 0),
                            stop=(h == 3),
                        )
                    sPE.inc(li, 1, m("a_dS", tt))
            pass
            if TRUNC < 3:
                return
            sDV.wait(t, "endA_dv")
            sAC.wait(t, "endA_ac")
            sDV.wait(t, m("a_p", NT - 1))
            for tt in range(NT):
                if tt >= 2:
                    sDV.wait(t, m("b1_u", tt - 2))
                for c in range(8):
                    li = t.matmul(
                        pM[tt % 2][:, 0:136],
                        xnfm_v(c, tt),
                        wgu[:, c * 136 : (c + 1) * 136],
                        start=(c == 0),
                        stop=(c == 7),
                    )
                sPE.inc(li, 1, m("b1_gu", tt))
                sDV.wait(t, m("b1_part", tt - 1))
                for c in range(8):
                    for half in range(2):
                        t.matmul(
                            pO[half][:],
                            xnfm_v(c, tt),
                            wff[:, c * 1024 + half * 512 : c * 1024 + (half + 1) * 512],
                            start=(c == 0),
                            stop=False,
                        )
                sDV.wait(t, m("b1_u", tt))
                if tt >= 2:
                    sAC.wait(t, m("b1_uTe", tt - 2))
                li = t.transpose(pT[tt % 2][:, 0:128], ubf[tt % 2][:], ident[:])
                sPE.inc(li, 1, m("b1_uT", tt))
                sAC.wait(t, m("b1_uTe", tt))
                for half in range(2):
                    li = t.matmul(
                        pO[half][:],
                        uT[tt % 2][:],
                        vfl[:, half * 512 : (half + 1) * 512],
                        start=False,
                        stop=True,
                    )
                sPE.inc(li, 1, m("b1_out", tt))
            pass
            if TRUNC < 4:
                return
            sDV.wait(t, "endB1_dv")
            sAC.wait(t, "endB1_ac")
            for tt in range(NT):
                if tt >= 2:
                    sAC.wait(t, m("b2_qelu", tt - 2))
                    sDV.wait(t, m("b2_p", tt - 2))
                for c in range(8):
                    li = t.matmul(
                        pM[tt % 2][:, 0:256],
                        xnfm_v(c, tt),
                        wqr[:, c * 256 : (c + 1) * 256],
                        start=(c == 0),
                        stop=(c == 7),
                    )
                sPE.inc(li, 1, m("b2_q", tt))
                sDV.wait(t, m("b2_qtm", tt))
                if tt >= 2:
                    sAC.wait(t, m("b2_yfev", tt - 2))
                for h in range(H):
                    li = t.transpose(
                        pT[tt % 2][0:64, 256 + h * 128 : 256 + (h + 1) * 128],
                        ktm[tt % 2][:, h * 64 : (h + 1) * 64],
                        ident[:],
                    )
                sPE.inc(li, 1, m("b2_qT", tt))
                if BXL < 2:
                    continue
                sAC.wait(t, m("b2_qfev", tt))
                if tt >= 2 and BXL >= 3:
                    sDV.wait(t, m("b2_am", tt - 2))
                for h in range(H):
                    li = t.matmul(
                        pA[tt % 2][:, h * 128 : (h + 1) * 128],
                        kfm_v(h, tt),
                        qfm[tt % 2][0:64, h * 128 : (h + 1) * 128],
                        start=(h == 0),
                        stop=(h == 3),
                    )
                sPE.inc(li, 1, m("b2_AT", tt))
                if BXL < 4:
                    continue
                sDV.wait(t, m("b2_am", tt))
                sDV.wait(t, m("b2_spk", tt))
                if BXL >= 6:
                    sDV.wait(t, m("b2_out", tt - 1))
                for h in range(H):
                    t.matmul(
                        pO[0][:, h * 65 : (h + 1) * 65],
                        ams[tt % 2][:, h * 128 : (h + 1) * 128],
                        vaug_v(tt, h),
                        start=True,
                        stop=False,
                    )
                    li = t.matmul(
                        pO[0][:, h * 65 : (h + 1) * 65],
                        qfm[tt % 2][0:64, h * 128 : (h + 1) * 128],
                        s_pk[tt % 2][0:64, h * 65 : (h + 1) * 65],
                        start=False,
                        stop=True,
                    )
                sPE.inc(li, 1, m("b2_num", tt))
                if BXL < 5:
                    continue
                sAC.wait(t, m("b2_y", tt))
                if tt >= 2:
                    sAC.wait(t, m("b2_yfev", tt - 2))
                for fc in range(2):
                    li = t.transpose(
                        pT[tt % 2][:, 512 + fc * 128 : 512 + (fc + 1) * 128],
                        ytm[tt % 2][:, fc * 128 : (fc + 1) * 128],
                        ident[:],
                    )
                sPE.inc(li, 1, m("b2_yT", tt))
                if BXL < 6:
                    continue
                sAC.wait(t, m("b2_yfev", tt))
                for half in range(2):
                    for fc in range(2):
                        li = t.matmul(
                            pO[half][:],
                            yfm[tt % 2][:, fc * 128 : (fc + 1) * 128],
                            wot[:, fc * 1024 + half * 512 : fc * 1024 + (half + 1) * 512],
                            start=(fc == 0),
                            stop=(fc == 1),
                        )
                sPE.inc(li, 1, m("b2_wo", tt))

        def emit_scalar(s):
            s = CEng(s, sAC)
            for i in range(NB):
                sPE.wait(s, f"wT{i}")
                nch = nchunks(i)
                dst = WBLK[i][1]
                if WBLK[i][2] == "g":
                    src = pT[i % 2][:, 0 : nch * 128].rearrange("p (c n) -> p c n", c=nch)[:, :, 0:8]
                else:
                    src = pT[i % 2][:, 0 : nch * 128].rearrange("p (c n) -> p c n", c=nch)
                sAC.inc(s.activation(dst, src, AF.Copy), 1, f"wev{i}")
            if TRUNC < 1:
                return
            sPL.wait(s, "setup_const")
            for tt in range(NT):
                sDV.wait(s, m("a_var", tt))
                if tt >= 2 and AXL >= 2:
                    sPL.wait(s, m("a_xn", tt - 2))
                s.activation(mvb[tt % 2][:, 4:5], mvb[tt % 2][:, 2:3], AF.Ln, bias=eps5[:, 0:1])
                sAC.inc(s.activation(mvb[tt % 2][:, 1:2], mvb[tt % 2][:, 4:5], AF.Exp, scale=-0.5), 1, m("a_rstd", tt))
                if AXL >= 2:
                    sPE.wait(s, m("a_xnT", tt))
                    s.nop(cycle_cnt=255)
                    s.nop(cycle_cnt=255)
                    sAC.inc(s.activation(xnfm_3d(tt), pT[tt % 2][:, 0:1024].rearrange("p (c n) -> p c n", c=8), AF.Copy), 1, m("a_xnev", tt))
                if AXL >= 4:
                    sPE.wait(s, m("a_kv", tt))
                    if tt >= 2:
                        sDV.wait(s, m("a_p", tt - 2))
                        if AXL >= 5:
                            sDV.wait(s, m("a_ktm", tt - 2))
                    s.activation(eA[tt % 2][:], pA[tt % 2][:, 0:256], AF.Relu)
                    s.activation(eB[tt % 2][:], pA[tt % 2][:, 0:256], AF.Relu, scale=-1.0)
                    sAC.inc(s.activation(eB[tt % 2][:], eB[tt % 2][:], AF.Exp, scale=-1.0), 1, m("a_kelu", tt))
                    sAC.inc(s.activation(vaug_hd(tt)[:, :, 0:64], pA[tt % 2][:, 256:512].rearrange("p (h c) -> p h c", h=4), AF.Copy), 1, m("a_vev", tt))
                    sDV.wait(s, m("a_kred", tt))
                    s.activation(nrm[tt % 2][:, 4:8], nrm[tt % 2][:, 0:4], AF.Ln)
                    sAC.inc(s.activation(nrm[tt % 2][:, 4:8], nrm[tt % 2][:, 4:8], AF.Exp, scale=0.5), 1, m("a_ksq", tt))
                if AXL >= 5:
                    sPE.wait(s, m("a_kT", tt))
                    s.nop(cycle_cnt=255)
                    s.nop(cycle_cnt=255)
                    sAC.inc(s.activation(kfm_3d(tt), pT[(tt + 1) % 2][0:64, 0:512].rearrange("p (c n) -> p c n", c=4), AF.Copy), 1, m("a_kfev", tt))
            sAC.mark_only("endA_ac")
            if TRUNC < 3:
                return
            for tt in range(NT):
                sDV.wait(s, m("b1_gmax", tt))
                sAC.inc(s.activation(gsc[tt % 2][:], pM[tt % 2][:, 0:8], AF.Exp, bias=mvb[tt % 2][:, 5:6]), 1, m("b1_ge", tt))
                sPE.wait(s, m("b1_uT", tt))
                if tt >= 2:
                    sPE.wait(s, m("b1_out", tt - 2))
                sAC.inc(s.activation(uT[tt % 2][:], pT[tt % 2][:, 0:128], AF.Copy), 1, m("b1_uTe", tt))
            sAC.mark_only("endB1_ac")
            if TRUNC < 4:
                return
            for tt in range(NT):
                sPE.wait(s, m("b2_q", tt))
                if tt >= 2:
                    sDV.wait(s, m("b2_p", tt - 2))
                    sDV.wait(s, m("b2_qtm", tt - 2))
                s.activation(eA[tt % 2][:], pM[tt % 2][:, 0:256], AF.Relu)
                s.activation(eB[tt % 2][:], pM[tt % 2][:, 0:256], AF.Relu, scale=-1.0)
                sAC.inc(s.activation(eB[tt % 2][:], eB[tt % 2][:], AF.Exp, scale=-1.0), 1, m("b2_qelu", tt))
                sDV.wait(s, m("b2_qred", tt))
                s.activation(nrm[tt % 2][:, 4:8], nrm[tt % 2][:, 0:4], AF.Ln)
                sAC.inc(s.activation(nrm[tt % 2][:, 4:8], nrm[tt % 2][:, 4:8], AF.Exp, scale=0.5), 1, m("b2_qsq", tt))
                sPE.wait(s, m("b2_qT", tt))
                if tt >= 2:
                    sPE.wait(s, m("b2_num", tt - 2))
                sAC.inc(s.activation(
                    qfm[tt % 2][0:64, :],
                    pT[tt % 2][0:64, 256:768],
                    AF.Copy,
                ), 1, m("b2_qfev", tt))
                if BXL < 4:
                    continue
                sPE.wait(s, m("b2_num", tt))
                if tt >= 2:
                    sDV.wait(s, m("b2_rc", tt - 2))
                for h in range(H):
                    li = s.activation(dss[tt % 2][:, h : h + 1], pO[0][:, h * 65 + 64 : h * 65 + 65], AF.Identity, bias=eps6[:, 0:1])
                sAC.inc(li, 1, m("b2_de", tt))
                if BXL < 5:
                    continue
                sDV.wait(s, m("b2_rc", tt))
                if tt >= 2:
                    sPE.wait(s, m("b2_yT", tt - 2))
                for h in range(H):
                    li = s.activation(
                        ytm[tt % 2][:, h * 64 : (h + 1) * 64],
                        pO[0][:, h * 65 : h * 65 + 64],
                        AF.Identity,
                        scale=dss[tt % 2][:, 4 + h : 5 + h],
                    )
                sAC.inc(li, 1, m("b2_y", tt))
                sPE.wait(s, m("b2_yT", tt))
                if tt >= 2:
                    sPE.wait(s, m("b2_wo", tt - 2))
                sAC.inc(s.activation(
                    yfm[tt % 2][:].rearrange("p (c n) -> p c n", c=2),
                    pT[tt % 2][:, 512:768].rearrange("p (c n) -> p c n", c=2),
                    AF.Copy,
                ), 1, m("b2_yfev", tt))
            sAC.mark_only("endB2_ac")

        def emit_vector(v):
            v = CEng(v, sDV)
            sPL.wait(v, "setup_const")
            dC.wait(v, "consts")
            if TRUNC < 1:
                return
            for tt in range(NT):
                dX[tt % 2].wait(v, m("ax", tt))
                v.bn_stats(st12[tt % 2][:, 0:6], xbuf[tt % 2][:, 0:512])
                sDV.inc(v.bn_stats(st12[tt % 2][:, 6:12], xbuf[tt % 2][:, 512:1024]), 1, m("a_bn", tt))
                sDV.wait(v, m("a_bn", tt))
                if tt >= 2:
                    sPL.wait(v, m("a_xn", tt - 2))
                    sAC.wait(v, m("a_rstd", tt - 2))
                v.bn_aggr(mvb[tt % 2][:, 4:6], st12[tt % 2][:, 0:6])
                sDV.inc(v.bn_aggr(mvb[tt % 2][:, 6:8], st12[tt % 2][:, 6:12]), 1, m("a_ag", tt))
                sDV.wait(v, m("a_ag", tt))
                v.tensor_tensor(mvb[tt % 2][:, 0:1], mvb[tt % 2][:, 4:5], mvb[tt % 2][:, 6:7], ALU.add)
                v.tensor_tensor(mvb[tt % 2][:, 2:3], mvb[tt % 2][:, 5:6], mvb[tt % 2][:, 7:8], ALU.add)
                sDV.inc(v.tensor_tensor(mvb[tt % 2][:, 3:4], mvb[tt % 2][:, 4:5], mvb[tt % 2][:, 6:7], ALU.subtract), 1, m("a_cmb", tt))
                sDV.wait(v, m("a_cmb", tt))
                v.tensor_scalar_mul(mvb[tt % 2][:, 0:1], mvb[tt % 2][:, 0:1], 0.5)
                v.tensor_scalar_mul(mvb[tt % 2][:, 2:3], mvb[tt % 2][:, 2:3], 0.5)
                sDV.inc(v.tensor_tensor(mvb[tt % 2][:, 3:4], mvb[tt % 2][:, 3:4], mvb[tt % 2][:, 3:4], ALU.mult), 1, m("a_cmb2", tt))
                sDV.wait(v, m("a_cmb2", tt))
                v.tensor_scalar_mul(mvb[tt % 2][:, 3:4], mvb[tt % 2][:, 3:4], 0.25)
                sDV.inc(
                    v.tensor_tensor(mvb[tt % 2][:, 2:3], mvb[tt % 2][:, 2:3], mvb[tt % 2][:, 3:4], ALU.add),
                    1,
                    m("a_var", tt),
                )
                if AXL >= 4:
                    sDV.mark_only(m("a_kmin", tt))
                    sDV.mark_only(m("a_krelu", tt))
                    sAC.wait(v, m("a_kelu", tt))
                    sDV.inc(v.tensor_tensor(eA[tt % 2][:], eA[tt % 2][:], eB[tt % 2][:], ALU.add), 1, m("a_p", tt))
                    sDV.inc(v.tensor_tensor(eB[tt % 2][:], eA[tt % 2][:], eA[tt % 2][:], ALU.mult), 1, m("a_ksqr", tt))
                    sDV.wait(v, m("a_ksqr", tt))
                    if tt >= 2:
                        sAC.wait(v, m("a_ksq", tt - 2))
                    sDV.inc(
                        v.tensor_reduce(
                            nrm[tt % 2][:, 0:4].unsqueeze(2),
                            eB[tt % 2][:].rearrange("p (a b) -> p a b", a=4),
                            AX.X,
                            ALU.add,
                        ),
                        1,
                        m("a_kred", tt),
                    )
                    sAC.wait(v, m("a_ksq", tt))
                    sDV.inc(v.tensor_scalar_add(nrm[tt % 2][:, 4:8], nrm[tt % 2][:, 4:8], EPS_PU), 1, m("a_keps", tt))
                    sDV.wait(v, m("a_keps", tt))
                    sDV.inc(v.reciprocal(nrm[tt % 2][:, 4:8], nrm[tt % 2][:, 4:8]), 1, m("a_krec", tt))
                if AXL >= 5:
                    sDV.wait(v, m("a_krec", tt))
                    if tt >= 2:
                        sPE.wait(v, m("a_dS", tt - 2))
                    for h in range(H):
                        li = v.tensor_scalar_mul(
                            ktm[tt % 2][:, h * 64 : (h + 1) * 64],
                            eA[tt % 2][:, h * 64 : (h + 1) * 64],
                            nrm[tt % 2][:, 4 + h : 5 + h],
                        )
                    sDV.inc(li, 1, m("a_ktm", tt))
                if AXL >= 6:
                    sPE.wait(v, m("a_dS", tt))
                    v.nop(cycle_cnt=255)
                    v.nop(cycle_cnt=255)
                    sDV.inc(v.tensor_tensor(s_run[0:64, :], s_run[0:64, :], pM[tt % 2][0:64, 0:260], ALU.add), 1, m("a_Sa", tt))
                    sDV.wait(v, m("a_Sa", tt))
                    sDV.inc(v.tensor_copy(pref_v(tt + 1), s_run[0:64, :]), 1, m("a_S", tt))
            sDV.mark_only("endA_dv")
            if TRUNC >= 3:
                _b1only = True
            else:
                _b1only = False
            for tt in (range(NT) if _b1only else []):
                sPE.wait(v, m("b1_gu", tt))
                if tt >= 2:
                    sAC.wait(v, m("b1_ge", tt - 2))
                sDV.inc(
                    v.reduce_max(mvb[tt % 2][:, 5:6], pM[tt % 2][:, 0:8], AX.X, negate=True),
                    1,
                    m("b1_gmax", tt),
                )
                sAC.wait(v, m("b1_ge", tt))
                sDV.inc(v.reduce_sum(mvb[tt % 2][:, 6:7], gsc[tt % 2][:], AX.X), 1, m("b1_gsum", tt))
                sDV.wait(v, m("b1_gsum", tt))
                sDV.inc(v.reciprocal(mvb[tt % 2][:, 6:7], mvb[tt % 2][:, 6:7]), 1, m("b1_grec", tt))
                sDV.wait(v, m("b1_grec", tt))
                sDV.inc(
                    v.tensor_scalar(gsc[tt % 2][:], gsc[tt % 2][:], mvb[tt % 2][:, 6:7], 1.0 / 16.0, ALU.mult, ALU.mult),
                    1,
                    m("b1_gn", tt),
                )
                sDV.wait(v, m("b1_gn", tt))
                if tt >= 2:
                    sPE.wait(v, m("b1_uT", tt - 2))
                for e in range(8):
                    li = v.tensor_scalar_mul(
                        ubf[tt % 2][:, e * 16 : (e + 1) * 16],
                        pM[tt % 2][:, 8 + e * 16 : 8 + (e + 1) * 16],
                        gsc[tt % 2][:, e : e + 1],
                    )
                sDV.inc(li, 1, m("b1_u", tt))
                sPE.wait(v, m("b1_out", tt))
                dX[tt % 2].wait(v, m("b1x", tt))
                if tt >= 2:
                    dPS[tt % 2].wait(v, m("b1st", tt - 2))
                v.tensor_tensor(pbuf[tt % 2][:, 0:512], pO[0][:], xbuf[tt % 2][:, 0:512], ALU.add)
                sDV.inc(
                    v.tensor_tensor(pbuf[tt % 2][:, 512:1024], pO[1][:], xbuf[tt % 2][:, 512:1024], ALU.add),
                    1,
                    m("b1_part", tt),
                )
            sDV.mark_only("endB1_dv")
            if TRUNC < 2:
                return
            dS.wait(v, "gath")
            sDV.inc(v.tensor_scalar_mul(s_prev[0:64, :], gath[0:64, :], rolet[0:64, 0:1]), 1, "sprev")
            sDV.mark_only("sprev_m")
            if TRUNC < 4:
                return
            for tt in range(NT):
                sDV.wait(v, "sprev")
                if BXL >= 4:
                    if tt >= 2:
                        sPE.wait(v, m("b2_num", tt - 2))
                    sDV.inc(v.tensor_tensor(s_pk[tt % 2][0:64, :], pref_v(tt), s_prev[0:64, :], ALU.add), 1, m("b2_spk", tt))
                sDV.mark_only(m("b2_qmin", tt))
                sDV.mark_only(m("b2_qrelu", tt))
                sAC.wait(v, m("b2_qelu", tt))
                sDV.inc(v.tensor_tensor(eA[tt % 2][:], eA[tt % 2][:], eB[tt % 2][:], ALU.add), 1, m("b2_p", tt))
                sDV.inc(v.tensor_tensor(eB[tt % 2][:], eA[tt % 2][:], eA[tt % 2][:], ALU.mult), 1, m("b2_qsqr", tt))
                sDV.wait(v, m("b2_qsqr", tt))
                if tt >= 2:
                    sAC.wait(v, m("b2_qsq", tt - 2))
                sDV.inc(
                    v.tensor_reduce(
                        nrm[tt % 2][:, 0:4].unsqueeze(2),
                        eB[tt % 2][:].rearrange("p (a b) -> p a b", a=4),
                        AX.X,
                        ALU.add,
                    ),
                    1,
                    m("b2_qred", tt),
                )
                sAC.wait(v, m("b2_qsq", tt))
                sDV.inc(v.tensor_scalar_add(nrm[tt % 2][:, 4:8], nrm[tt % 2][:, 4:8], EPS_PU), 1, m("b2_qeps", tt))
                sDV.wait(v, m("b2_qeps", tt))
                sDV.inc(v.reciprocal(nrm[tt % 2][:, 4:8], nrm[tt % 2][:, 4:8]), 1, m("b2_qrec", tt))
                sDV.wait(v, m("b2_qrec", tt))
                if tt >= 2:
                    sPE.wait(v, m("b2_qT", tt - 2))
                for h in range(H):
                    li = v.tensor_scalar_mul(
                        ktm[tt % 2][:, h * 64 : (h + 1) * 64],
                        eA[tt % 2][:, h * 64 : (h + 1) * 64],
                        nrm[tt % 2][:, 4 + h : 5 + h],
                    )
                sDV.inc(li, 1, m("b2_qtm", tt))
                if BXL < 3:
                    continue
                sPE.wait(v, m("b2_AT", tt))
                if tt >= 2 and BXL >= 4:
                    sPE.wait(v, m("b2_num", tt - 2))
                for h in range(H):
                    li = v.tensor_tensor(
                        ams[tt % 2][:, h * 128 : (h + 1) * 128],
                        pA[tt % 2][:, h * 128 : (h + 1) * 128],
                        mskt[:],
                        ALU.mult,
                    )
                sDV.inc(li, 1, m("b2_am", tt))
                if BXL < 5:
                    continue
                sAC.wait(v, m("b2_de", tt))
                sDV.inc(v.reciprocal(dss[tt % 2][:, 4:8], dss[tt % 2][:, 0:4]), 1, m("b2_rc", tt))
                if BXL < 6:
                    continue
                sPE.wait(v, m("b2_wo", tt))
                dPL[tt % 2].wait(v, m("b2pl", tt))
                if tt >= 2:
                    dO[tt % 2].wait(v, m("ost", tt - 2))
                else:
                    dPS[tt % 2].wait(v, m("b1st", NT - 2 + tt))
                v.tensor_tensor(pbuf[tt % 2][:, 0:512], pO[0][:], xbuf[tt % 2][:, 0:512], ALU.add)
                sDV.inc(
                    v.tensor_tensor(pbuf[tt % 2][:, 512:1024], pO[1][:], xbuf[tt % 2][:, 512:1024], ALU.add),
                    1,
                    m("b2_out", tt),
                )
            sDV.mark_only("endB2_dv")

        _COUNTING[0] = True
        for f in (emit_sync, emit_gpsimd, emit_tensor, emit_scalar, emit_vector):
            f(_FakeEng())
        _COUNTING[0] = False
        for c in (sPE, sDV, sAC, sPL, dC, dS, *dX, *dW, *dPS, *dPL, *dO):
            c.n = 0
        block.sync(emit_sync)
        block.gpsimd(emit_gpsimd)
        block.tensor(emit_tensor)
        block.scalar(emit_scalar)
        block.vector(emit_vector)

    return nc


_NC_CACHE = {}


def kernel(
    L0_attn_x,
    L0_attn_Wq,
    L0_attn_Wk,
    L0_attn_Wv,
    L0_attn_Wo,
    L0_ffn_weight,
    L0_ffn_gate_w,
    L0_ffn_lora_U,
    L0_ffn_lora_V,
):
    x = np.ascontiguousarray(np.asarray(L0_attn_x, np.float32))
    W = {
        "Wq": np.ascontiguousarray(np.asarray(L0_attn_Wq, np.float32)),
        "Wk": np.ascontiguousarray(np.asarray(L0_attn_Wk, np.float32)),
        "Wv": np.ascontiguousarray(np.asarray(L0_attn_Wv, np.float32)),
        "Wo": np.ascontiguousarray(np.asarray(L0_attn_Wo, np.float32)),
        "Wff": np.ascontiguousarray(np.asarray(L0_ffn_weight, np.float32)),
        "gw": np.ascontiguousarray(np.asarray(L0_ffn_gate_w, np.float32)),
        "U2": np.ascontiguousarray(np.asarray(L0_ffn_lora_U, np.float32).reshape(128, 1024)),
        "loraV": np.ascontiguousarray(np.asarray(L0_ffn_lora_V, np.float32)),
    }
    idn = np.eye(128, dtype=ml_dtypes.bfloat16)
    mask = np.triu(np.ones((128, 128), np.float32))
    if "nc" not in _NC_CACHE:
        _NC_CACHE["nc"] = build_nc()
    nc = _NC_CACHE["nc"]
    in_maps = []
    for c in range(8):
        b, half = c // 2, c % 2
        in_maps.append(
            {
                "x": np.ascontiguousarray(x[b, half * 4096 : (half + 1) * 4096, :]),
                "Wq": W["Wq"], "Wk": W["Wk"], "Wv": W["Wv"], "Wo": W["Wo"],
                "Wff": W["Wff"], "gw": W["gw"], "U2": W["U2"], "loraV": W["loraV"],
                "idn": idn, "maskUT": mask,
                "role": np.full((128, 1), float(half), np.float32),
            }
        )
    res = run_bass_kernel_spmd(nc, in_maps, core_ids=list(range(8)))
    out = np.empty((4, 8192, 1024), np.float32)
    for c in range(8):
        b, half = c // 2, c % 2
        out[b, half * 4096 : (half + 1) * 4096, :] = res.results[c]["out"]
    if KDEBUG:
        kernel._last = res
    return out


# version 04
# speedup vs baseline: 1.1565x; 1.1565x over previous
import contextlib
import numpy as np
import ml_dtypes
import concourse.bass as bass
import concourse.mybir as mybir
from concourse.bass_utils import run_bass_kernel_spmd

F32 = mybir.dt.float32
BF16 = mybir.dt.bfloat16
AF = mybir.ActivationFunctionType
ALU = mybir.AluOpType
AX = mybir.AxisListType

NT = 32
L = 128
DM = 1024
H = 4
EPS_LN = 1e-5
EPS_PU = 1e-6
EPS_DEN = 1e-6
MAGIC = 1234.5

KDEBUG = False
RACE_DETECT = False
DEV_OWN_SLOT = False
TRUNC = 4
AXL = 6
BXL = 6


_COUNTING = [False]


class Ctr:
    def __init__(self, handle):
        self.h = handle
        self.n = 0
        self.marks = {}

    def inc(self, instr, k=1, mark=None):
        if getattr(self, "_chained_last", None) is not instr:
            instr.then_inc(self.h, k)
            self.n += k
        if mark is not None:
            if _COUNTING[0]:
                assert mark not in self.marks, mark
                self.marks[mark] = self.n
            else:
                assert self.marks.get(mark) == self.n, (mark, self.marks.get(mark), self.n)
        return self.n

    def mark_only(self, mark):
        if _COUNTING[0]:
            assert mark not in self.marks, mark
            self.marks[mark] = self.n
        else:
            assert self.marks.get(mark) == self.n, (mark, self.marks.get(mark), self.n)

    def wait(self, eng, mark):
        if mark is None:
            return
        if _COUNTING[0]:
            return
        eng.wait_ge(self.h, self.marks[mark])


class _FakeInstr:
    def then_inc(self, *a, **k):
        return self


class _FakeEng:
    def switch_core_id(self, n):
        return [_FakeArm(i) for i in range(n)]

    def __getattr__(self, name):
        def fn(*a, **k):
            return _FakeInstr()

        return fn


class _FakeArm(int):
    pass


CHAIN_OPS = {
    "activation", "tensor_tensor", "tensor_scalar", "tensor_scalar_mul",
    "tensor_scalar_add", "tensor_scalar_min", "tensor_scalar_max", "tensor_copy",
    "bn_stats", "bn_aggr", "reduce_max", "reduce_sum", "tensor_reduce",
    "reciprocal", "scalar_tensor_tensor", "memset", "tensor_add", "tensor_sub",
    "tensor_mul",
}


class CEng:

    def __init__(self, eng, ctr):
        self._e = eng
        self._c = ctr
        self._first = True

    WAIT_OPS = {
        "bn_aggr", "bn_stats", "reduce_max", "reduce_sum", "tensor_reduce",
        "reciprocal", "tensor_copy", "memset",
    }

    def __getattr__(self, nm):
        fn = getattr(self._e, nm)
        if nm in CHAIN_OPS:
            need_wait = nm in CEng.WAIT_OPS
            def wrapped(*a, **k):
                if need_wait and not self._first:
                    self._e.wait_ge(self._c.h, self._c.n)
                self._first = False
                ins = fn(*a, **k)
                ins.then_inc(self._c.h, 1)
                self._c.n += 1
                self._c._chained_last = ins
                return ins

            return wrapped
        return fn


def build_nc():
    nc = bass.Bass("TRN2", target_bir_lowering=False, debug=False, num_devices=8, detect_race_conditions=RACE_DETECT)

    x_d = nc.dram_tensor("x", [NT * L, DM], F32, kind="ExternalInput").ap()
    wq_d = nc.dram_tensor("Wq", [256, 1024], F32, kind="ExternalInput").ap()
    wk_d = nc.dram_tensor("Wk", [256, 1024], F32, kind="ExternalInput").ap()
    wv_d = nc.dram_tensor("Wv", [256, 1024], F32, kind="ExternalInput").ap()
    wo_d = nc.dram_tensor("Wo", [1024, 256], F32, kind="ExternalInput").ap()
    wf_d = nc.dram_tensor("Wff", [1024, 1024], F32, kind="ExternalInput").ap()
    gw_d = nc.dram_tensor("gw", [8, 1024], F32, kind="ExternalInput").ap()
    u2_d = nc.dram_tensor("U2", [128, 1024], F32, kind="ExternalInput").ap()
    lv_d = nc.dram_tensor("loraV", [8, 1024, 16], F32, kind="ExternalInput").ap()
    idn_d = nc.dram_tensor("idn", [128, 128], BF16, kind="ExternalInput").ap()
    msk_d = nc.dram_tensor("maskUT", [128, 128], F32, kind="ExternalInput").ap()
    role_d = nc.dram_tensor("role", [128, 1], F32, kind="ExternalInput").ap()
    out_d = nc.dram_tensor("out", [NT * L, DM], F32, kind="ExternalOutput").ap()

    part_d = nc.dram_tensor("partial", [NT * L, DM], F32)
    shS = nc.dram_tensor("shS", [2 * 64, 260], F32, addr_space="Shared")
    shF = nc.dram_tensor("shF", [1, 8], F32, addr_space="Shared")

    dbg = {}
    if KDEBUG:
        for nm, shp in [
            ("dbg_S", [64, 260]), ("dbg_sprev", [64, 260]),
            ("dbg_xn", [128, DM]), ("dbg_ktm", [128, 256]),
            ("dbg_g", [128, 8]), ("dbg_u", [128, 128]),
        ]:
            dbg[nm] = nc.dram_tensor(nm, shp, F32, kind="ExternalOutput").ap()
        dbg["dbg_pref"] = nc.dram_tensor("dbg_pref", [64, (NT + 1) * 260], BF16, kind="ExternalOutput").ap()
        dbg["dbg_kfm"] = nc.dram_tensor("dbg_kfm", [64, 4 * NT * L], BF16, kind="ExternalOutput").ap()
        dbg["dbg_vaug"] = nc.dram_tensor("dbg_vaug", [128, NT * 260], BF16, kind="ExternalOutput").ap()

    es = contextlib.ExitStack()
    E = es.enter_context
    with es:
        xbuf = [E(nc.sbuf_tensor(f"xbuf{i}", [128, DM], F32)) for i in range(2)]
        pbuf = [E(nc.sbuf_tensor(f"pbuf{i}", [128, DM], F32)) for i in range(2)]
        xnfm = E(nc.sbuf_tensor("xnfm", [128, 8 * NT * L], BF16))
        kfm = E(nc.sbuf_tensor("kfm", [64, 4 * NT * L], BF16))
        vaug = E(nc.sbuf_tensor("vaug", [128, NT * 260], BF16))
        prefix = E(nc.sbuf_tensor("prefix", [64, (NT + 1) * 260], BF16))
        s_run = E(nc.sbuf_tensor("s_run", [64, 260], F32))
        s_prev = E(nc.sbuf_tensor("s_prev", [64, 260], BF16))
        s_pk = [E(nc.sbuf_tensor(f"s_pk{i}", [64, 260], BF16)) for i in range(2)]
        gath = E(nc.sbuf_tensor("gath", [64, 260], F32))
        wkv = E(nc.sbuf_tensor("wkv", [128, 8 * 512], BF16))
        wgu = E(nc.sbuf_tensor("wgu", [128, 8 * 136], BF16))
        wqr = E(nc.sbuf_tensor("wqr", [128, 8 * 256], BF16))
        wff = E(nc.sbuf_tensor("wff", [128, 8 * 1024], BF16))
        wot = E(nc.sbuf_tensor("wot", [128, 2 * 1024], BF16))
        vfl = E(nc.sbuf_tensor("vfl", [128, 1024], BF16))
        stage = [E(nc.sbuf_tensor(f"stage{i}", [128, 1024], BF16)) for i in range(2)]
        eA = [E(nc.sbuf_tensor(f"eA{i}", [128, 256], F32)) for i in range(2)]
        eB = [E(nc.sbuf_tensor(f"eB{i}", [128, 256], F32)) for i in range(2)]
        ktm = [E(nc.sbuf_tensor(f"ktm{i}", [128, 256], BF16)) for i in range(2)]
        qfm = [E(nc.sbuf_tensor(f"qfm{i}", [64, 512], BF16)) for i in range(2)]
        ams = [E(nc.sbuf_tensor(f"ams{i}", [128, 512], BF16)) for i in range(2)]
        ytm = [E(nc.sbuf_tensor(f"ytm{i}", [128, 256], BF16)) for i in range(2)]
        yfm = [E(nc.sbuf_tensor(f"yfm{i}", [128, 256], BF16)) for i in range(2)]
        ubf = [E(nc.sbuf_tensor(f"ubf{i}", [128, 128], BF16)) for i in range(2)]
        uT = [E(nc.sbuf_tensor(f"uT{i}", [128, 128], BF16)) for i in range(2)]
        gsc = [E(nc.sbuf_tensor(f"gsc{i}", [128, 8], F32)) for i in range(2)]
        st12 = [E(nc.sbuf_tensor(f"st12{i}", [128, 12], F32)) for i in range(2)]
        mvb = [E(nc.sbuf_tensor(f"mv{i}", [128, 8], F32)) for i in range(2)]
        nrm = [E(nc.sbuf_tensor(f"nrm{i}", [128, 8], F32)) for i in range(2)]
        dss = [E(nc.sbuf_tensor(f"dss{i}", [128, 8], F32)) for i in range(2)]
        ident = E(nc.sbuf_tensor("ident", [128, 128], BF16))
        mskt = E(nc.sbuf_tensor("mskt", [128, 128], F32))
        rolet = E(nc.sbuf_tensor("rolet", [128, 1], F32))
        eps6 = E(nc.sbuf_tensor("eps6", [128, 1], F32))
        eps5 = E(nc.sbuf_tensor("eps5", [128, 1], F32))
        flagsrc = E(nc.sbuf_tensor("flagsrc", [1, 8], F32))
        pA = [E(nc.psum_tensor(f"pA{i}", [128, 512], F32)) for i in range(2)]
        pO = [E(nc.psum_tensor(f"pO{i}", [128, 512], F32)) for i in range(2)]
        pM = [E(nc.psum_tensor(f"pM{i}", [128, 512], F32)) for i in range(2)]
        pT = [E(nc.psum_tensor(f"pT{i}", [128, 1024], BF16)) for i in range(2)]

        sPE = Ctr(E(nc.semaphore("sPE")))
        sDV = Ctr(E(nc.semaphore("sDV")))
        sAC = Ctr(E(nc.semaphore("sAC")))
        sPL = Ctr(E(nc.semaphore("sPL")))
        dX = [Ctr(E(nc.semaphore(f"dX{i}"))) for i in range(2)]
        dW = [Ctr(E(nc.semaphore(f"dW{i}"))) for i in range(2)]
        dPS = [Ctr(E(nc.semaphore(f"dPS{i}"))) for i in range(2)]
        dPL = [Ctr(E(nc.semaphore(f"dPL{i}"))) for i in range(2)]
        dO = [Ctr(E(nc.semaphore(f"dO{i}"))) for i in range(2)]
        dC = Ctr(E(nc.semaphore("dC")))
        dS = Ctr(E(nc.semaphore("dS")))

        block = E(nc.Block())

        def m(name, t):
            return f"{name}.{t}" if t >= 0 else None

        def xnfm_v(c, t):
            return xnfm[:, c * NT * L + t * L : c * NT * L + (t + 1) * L]

        def xnfm_3d(t):
            return xnfm[:].rearrange("p (c n) -> p c n", c=8)[:, :, t * L : (t + 1) * L]

        def kfm_v(h, t):
            return kfm[0:64, h * NT * L + t * L : h * NT * L + (t + 1) * L]

        def kfm_3d(t):
            return kfm[0:64, :].rearrange("p (c n) -> p c n", c=4)[:, :, t * L : (t + 1) * L]

        def vaug_v(t, h):
            return vaug[:, t * 260 + h * 65 : t * 260 + (h + 1) * 65]

        def vaug_hd(t):
            return vaug[:, t * 260 : (t + 1) * 260].rearrange("p (h c) -> p h c", h=4)

        def pref_v(t):
            return prefix[0:64, t * 260 : (t + 1) * 260]

        WBLK = []
        wkv3 = wkv[:].rearrange("p (c n) -> p c n", c=8)
        wqr3 = wqr[:].rearrange("p (c n) -> p c n", c=8)
        wff3 = wff[:].rearrange("p (c n) -> p c n", c=8)
        wgu3 = wgu[:].rearrange("p (c n) -> p c n", c=8)
        wot3 = wot[:].rearrange("p (c n) -> p c n", c=2)
        vfl3 = vfl[:].rearrange("p (c n) -> p c n", c=1)
        for rb in range(2):
            WBLK.append((wk_d[rb * 128 : (rb + 1) * 128, :], wkv3[:, :, rb * 128 : rb * 128 + 128], "n"))
        for rb in range(2):
            WBLK.append((wv_d[rb * 128 : (rb + 1) * 128, :], wkv3[:, :, 256 + rb * 128 : 256 + rb * 128 + 128], "n"))
        for rb in range(2):
            WBLK.append((wq_d[rb * 128 : (rb + 1) * 128, :], wqr3[:, :, rb * 128 : rb * 128 + 128], "n"))
        for rb in range(8):
            WBLK.append((wf_d[rb * 128 : (rb + 1) * 128, :], wff3[:, :, rb * 128 : rb * 128 + 128], "n"))
        WBLK.append((gw_d[:, :], wgu3[:, :, 0:8], "g"))
        WBLK.append((u2_d[:, :], wgu3[:, :, 8:136], "n"))
        for rb in range(8):
            WBLK.append((wo_d[rb * 128 : (rb + 1) * 128, :], wot3[:, :, rb * 128 : rb * 128 + 128], "n"))
        lv4 = lv_d.rearrange("e (oc o) r -> oc o e r", o=128)
        for oc in range(8):
            WBLK.append((lv4[oc], vfl3[:, :, oc * 128 : oc * 128 + 128], "v"))
        NB = len(WBLK)

        def nchunks(i):
            src, _, kind = WBLK[i]
            if kind == "g":
                return 8
            if kind == "v":
                return 1
            return src.shape[1] // 128

        def emit_sync(y):
            y.dma_start(out=ident[:], in_=idn_d).then_inc(dC.h, 16)
            y.dma_start(out=mskt[:], in_=msk_d).then_inc(dC.h, 16)
            y.dma_start(out=rolet[:], in_=role_d).then_inc(dC.h, 16)
            dC.n += 48
            dC.mark_only("consts")
            if TRUNC >= 1:
                for t in range(NT):
                    if t >= 2:
                        sDV.wait(y, m("a_var", t - 2))
                        if AXL >= 2:
                            sPL.wait(y, m("a_xn", t - 2))
                    dX[t % 2].inc(y.dma_start(out=xbuf[t % 2][:], in_=x_d[t * L : (t + 1) * L, :]), 16, m("ax", t))
            if TRUNC < 3:
                if TRUNC >= 1:
                    sDV.wait(y, "endA_dv")
                    if AXL >= 5:
                        sAC.wait(y, m("a_kfev", NT - 1))
                    if AXL >= 4:
                        sDV.wait(y, m("a_p", NT - 1))
                else:
                    sAC.wait(y, f"wev{NB - 1}")
                if TRUNC >= 2:
                    sDV.wait(y, "sprev")
                dO[0].inc(y.dma_start(out=out_d[0:128, :], in_=pbuf[0][:]), 16)
                y.wait_ge(dO[0].h, dO[0].n)
                return
            sDV.wait(y, "endA_dv")
            for t in range(NT):
                if t >= 2:
                    sDV.wait(y, m("b1_part", t - 2))
                else:
                    sPL.wait(y, m("a_xn", NT - 2 + t))
                dX[t % 2].inc(y.dma_start(out=xbuf[t % 2][:], in_=x_d[t * L : (t + 1) * L, :]), 16, m("b1x", t))
                sDV.wait(y, m("b1_part", t))
                dPS[t % 2].inc(y.dma_start(out=part_d.ap()[t * L : (t + 1) * L, :], in_=pbuf[t % 2][:]), 16, m("b1st", t))
            if TRUNC < 4:
                sDV.wait(y, "endB1_dv")
                dPS[(NT - 1) % 2].wait(y, m("b1st", NT - 1))
                dPS[(NT - 2) % 2].wait(y, m("b1st", NT - 2))
                dO[0].inc(y.dma_start(out=out_d[0:128, :], in_=pbuf[0][:]), 16)
                y.wait_ge(dO[0].h, dO[0].n)
                return
            sDV.wait(y, "endB1_dv")
            if BXL < 6:
                ends = {1: ("sAC", "b2_qfev"), 2: ("sPE", "b2_AT"), 3: ("sDV", "b2_am"), 4: ("sPE", "b2_num"), 5: ("sAC", "b2_yfev")}
                sem, mk = ends[BXL]
                {"sAC": sAC, "sPE": sPE, "sDV": sDV}[sem].wait(y, m(mk, NT - 1))
                dO[0].inc(y.dma_start(out=out_d[0:128, :], in_=pbuf[0][:]), 16)
                y.wait_ge(dO[0].h, dO[0].n)
                return
            for t in range(NT):
                if t >= 2:
                    sDV.wait(y, m("b2_out", t - 2))
                dPS[t % 2].wait(y, m("b1st", t))
                dPL[t % 2].inc(y.dma_start(out=xbuf[t % 2][:], in_=part_d.ap()[t * L : (t + 1) * L, :]), 16, m("b2pl", t))
                sDV.wait(y, m("b2_out", t))
                dO[t % 2].inc(y.dma_start(out=out_d[t * L : (t + 1) * L, :], in_=pbuf[t % 2][:]), 16, m("ost", t))
            if KDEBUG:
                sDV.wait(y, "endB2_dv")
                dO[0].inc(y.dma_start(out=dbg["dbg_S"], in_=s_run[0:64, :]), 16)
                dO[1].inc(y.dma_start(out=dbg["dbg_sprev"], in_=gath[0:64, :]), 16)
                dO[0].inc(y.dma_start(out=dbg["dbg_pref"], in_=prefix[0:64, :]), 16)
                dO[0].inc(y.dma_start(out=dbg["dbg_kfm"], in_=kfm[0:64, :]), 16)
                dO[0].inc(y.dma_start(out=dbg["dbg_vaug"], in_=vaug[:, :]), 16)
            y.wait_ge(dO[0].h, dO[0].n)
            y.wait_ge(dO[1].h, dO[1].n)

        def emit_gpsimd(g):
            g = CEng(g, sPL)
            g.memset(eps6[:], EPS_DEN)
            g.memset(eps5[:], EPS_LN)
            g.memset(flagsrc[:], MAGIC)
            g.memset(s_run[0:64, :], 0.0)
            g.memset(pref_v(0), 0.0)
            for t_ in range(NT):
                for h in range(H):
                    li_ms = g.memset(vaug[:, t_ * 260 + h * 65 + 64 : t_ * 260 + h * 65 + 65], 1.0)
            sPL.inc(li_ms, 1, "setup_const")
            for i in range(NB):
                if i >= 2:
                    sPE.wait(g, f"wT{i - 2}")
                kind = WBLK[i][2]
                if kind == "v":
                    dst = stage[i % 2][:, 0:128].rearrange("p (e r) -> p e r", e=8)
                else:
                    src = WBLK[i][0]
                    dst = stage[i % 2][0 : src.shape[0], 0 : src.shape[1]]
                dW[i % 2].inc(g.dma_start(out=dst, in_=WBLK[i][0]), 16, f"wld{i}")
            if TRUNC < 1:
                return
            sPE.wait(g, f"wT{NB - 1}")
            for t in range(NT):
                if AXL >= 2:
                    dX[t % 2].wait(g, m("ax", t))
                    sAC.wait(g, m("a_rstd", t))
                    if t >= 2:
                        sPE.wait(g, m("a_xnT", t - 2))
                    sPL.inc(
                        g.tensor_scalar(
                            stage[t % 2][:, 0:1024],
                            xbuf[t % 2][:],
                            mvb[t % 2][:, 0:1],
                            mvb[t % 2][:, 1:2],
                            ALU.subtract,
                            ALU.mult,
                        ),
                        1,
                        m("a_xn", t),
                    )
                pass
            if TRUNC < 2:
                return
            sDV.wait(g, m("a_S", NT - 1))
            for arm in g.switch_core_id(128):
                par = int(arm) % 2
                g.dma_start(out=shS.ap()[par * 64 : par * 64 + 64, :], in_=s_run[0:64, :]).then_inc(dS.h, 16)
            dS.n += 16
            g.wait_ge(dS.h, dS.n)
            for arm in g.switch_core_id(128):
                par = int(arm) % 2
                g.dma_start(out=shF.ap()[0:1, par * 4 : par * 4 + 4], in_=flagsrc[0:1, 0:4]).then_inc(dS.h, 16)
            dS.n += 16
            if TRUNC >= 3:
                sDV.wait(g, m("b1_part", max(0, NT - 8)))
            for arm in g.switch_core_id(128):
                par = int(arm) % 2
                rdpar = par if DEV_OWN_SLOT else (1 - par)
                g.dma_start(out=gath[0:64, :], in_=shS.ap()[rdpar * 64 : rdpar * 64 + 64, :]).then_inc(dS.h, 16)
            dS.n += 16
            dS.mark_only("gath")
            if TRUNC < 4:
                return
            pass

        def emit_tensor(t):
            for i in range(NB):
                dW[i % 2].wait(t, f"wld{i}")
                if i == 0:
                    dC.wait(t, "consts")
                if i >= 2:
                    sAC.wait(t, f"wev{i - 2}")
                if WBLK[i][2] == "g":
                    for c in range(8):
                        li = t.transpose(
                            pT[i % 2][:, c * 128 : c * 128 + 8],
                            stage[i % 2][0:8, c * 128 : c * 128 + 128],
                            ident[0:8, 0:8],
                        )
                else:
                    for c in range(nchunks(i)):
                        li = t.transpose(
                            pT[i % 2][:, c * 128 : c * 128 + 128],
                            stage[i % 2][:, c * 128 : c * 128 + 128],
                            ident[:],
                        )
                sPE.inc(li, 1, f"wT{i}")
            if TRUNC < 1:
                return
            sPL.wait(t, "setup_const")
            for tt in range(NT):
                if AXL >= 2:
                    sPL.wait(t, m("a_xn", tt))
                    if tt >= 2:
                        sAC.wait(t, m("a_xnev", tt - 2))
                    if AXL >= 5:
                        sAC.wait(t, m("a_kfev", tt - 1))
                    xn_tm = stage[tt % 2]
                    for c in range(8):
                        li = t.transpose(pT[tt % 2][:, c * 128 : c * 128 + 128], xn_tm[:, c * 128 : c * 128 + 128], ident[:])
                    sPE.inc(li, 1, m("a_xnT", tt))
                if AXL >= 3:
                    sAC.wait(t, m("a_xnev", tt))
                    if tt >= 2:
                        if AXL >= 4:
                            sAC.wait(t, m("a_vev", tt - 2))
                            sAC.wait(t, m("a_kelu", tt - 2))
                            sDV.wait(t, m("a_p", tt - 2))
                    for c in range(8):
                        li = t.matmul(
                            pA[tt % 2][:],
                            xnfm_v(c, tt),
                            wkv[:, c * 512 : (c + 1) * 512],
                            start=(c == 0),
                            stop=(c == 7),
                        )
                    sPE.inc(li, 1, m("a_kv", tt))
                if AXL >= 5:
                    sDV.wait(t, m("a_ktm", tt))
                    sAC.wait(t, m("a_xnev", tt - 1))
                    if tt >= 2:
                        sAC.wait(t, m("a_kfev", tt - 2))
                    for h in range(H):
                        li = t.transpose(
                            pT[(tt + 1) % 2][0:64, h * 128 : (h + 1) * 128],
                            ktm[tt % 2][:, h * 64 : (h + 1) * 64],
                            ident[:],
                        )
                    sPE.inc(li, 1, m("a_kT", tt))
                if AXL >= 6:
                    sAC.wait(t, m("a_vev", tt))
                    if tt >= 2:
                        sDV.wait(t, m("a_S", tt - 2))
                    for h in range(H):
                        li = t.matmul(
                            pM[tt % 2][0:64, h * 65 : (h + 1) * 65],
                            ktm[tt % 2][:, h * 64 : (h + 1) * 64],
                            vaug_v(tt, h),
                            start=(h == 0),
                            stop=(h == 3),
                        )
                    sPE.inc(li, 1, m("a_dS", tt))
            pass
            if TRUNC < 3:
                return
            sDV.wait(t, "endA_dv")
            sAC.wait(t, "endA_ac")
            sDV.wait(t, m("a_p", NT - 1))
            for tt in range(NT):
                if tt >= 2:
                    sDV.wait(t, m("b1_u", tt - 2))
                for c in range(8):
                    li = t.matmul(
                        pM[tt % 2][:, 0:136],
                        xnfm_v(c, tt),
                        wgu[:, c * 136 : (c + 1) * 136],
                        start=(c == 0),
                        stop=(c == 7),
                    )
                sPE.inc(li, 1, m("b1_gu", tt))
                if tt >= 2:
                    sDV.wait(t, m("b1_part", tt - 2))
                oB = pO if tt % 2 == 0 else pA
                for c in range(8):
                    for half in range(2):
                        t.matmul(
                            oB[half][:],
                            xnfm_v(c, tt),
                            wff[:, c * 1024 + half * 512 : c * 1024 + (half + 1) * 512],
                            start=(c == 0),
                            stop=False,
                        )
                sDV.wait(t, m("b1_u", tt))
                if tt >= 2:
                    sAC.wait(t, m("b1_uTe", tt - 2))
                li = t.transpose(pT[tt % 2][:, 0:128], ubf[tt % 2][:], ident[:])
                sPE.inc(li, 1, m("b1_uT", tt))
                sAC.wait(t, m("b1_uTe", tt))
                for half in range(2):
                    li = t.matmul(
                        oB[half][:],
                        uT[tt % 2][:],
                        vfl[:, half * 512 : (half + 1) * 512],
                        start=False,
                        stop=True,
                    )
                sPE.inc(li, 1, m("b1_out", tt))
            pass
            if TRUNC < 4:
                return
            sDV.wait(t, "endB1_dv")
            sAC.wait(t, "endB1_ac")
            for tt in range(NT):
                if tt >= 2:
                    sAC.wait(t, m("b2_qelu", tt - 2))
                    sDV.wait(t, m("b2_p", tt - 2))
                for c in range(8):
                    li = t.matmul(
                        pM[tt % 2][:, 0:256],
                        xnfm_v(c, tt),
                        wqr[:, c * 256 : (c + 1) * 256],
                        start=(c == 0),
                        stop=(c == 7),
                    )
                sPE.inc(li, 1, m("b2_q", tt))
                sDV.wait(t, m("b2_qtm", tt))
                if tt >= 2:
                    sAC.wait(t, m("b2_yfev", tt - 2))
                for h in range(H):
                    li = t.transpose(
                        pT[tt % 2][0:64, 256 + h * 128 : 256 + (h + 1) * 128],
                        ktm[tt % 2][:, h * 64 : (h + 1) * 64],
                        ident[:],
                    )
                sPE.inc(li, 1, m("b2_qT", tt))
                if BXL < 2:
                    continue
                sAC.wait(t, m("b2_qfev", tt))
                if tt >= 2 and BXL >= 3:
                    sDV.wait(t, m("b2_am", tt - 2))
                for h in range(H):
                    li = t.matmul(
                        pA[tt % 2][:, h * 128 : (h + 1) * 128],
                        kfm_v(h, tt),
                        qfm[tt % 2][0:64, h * 128 : (h + 1) * 128],
                        start=(h == 0),
                        stop=(h == 3),
                    )
                sPE.inc(li, 1, m("b2_AT", tt))
                if BXL < 4:
                    continue
                sDV.wait(t, m("b2_am", tt))
                sDV.wait(t, m("b2_spk", tt))
                if BXL >= 6:
                    sDV.wait(t, m("b2_out", tt - 1))
                for h in range(H):
                    t.matmul(
                        pO[0][:, h * 65 : (h + 1) * 65],
                        ams[tt % 2][:, h * 128 : (h + 1) * 128],
                        vaug_v(tt, h),
                        start=True,
                        stop=False,
                    )
                    li = t.matmul(
                        pO[0][:, h * 65 : (h + 1) * 65],
                        qfm[tt % 2][0:64, h * 128 : (h + 1) * 128],
                        s_pk[tt % 2][0:64, h * 65 : (h + 1) * 65],
                        start=False,
                        stop=True,
                    )
                sPE.inc(li, 1, m("b2_num", tt))
                if BXL < 5:
                    continue
                sAC.wait(t, m("b2_y", tt))
                if tt >= 2:
                    sAC.wait(t, m("b2_yfev", tt - 2))
                for fc in range(2):
                    li = t.transpose(
                        pT[tt % 2][:, 512 + fc * 128 : 512 + (fc + 1) * 128],
                        ytm[tt % 2][:, fc * 128 : (fc + 1) * 128],
                        ident[:],
                    )
                sPE.inc(li, 1, m("b2_yT", tt))
                if BXL < 6:
                    continue
                sAC.wait(t, m("b2_yfev", tt))
                for half in range(2):
                    for fc in range(2):
                        li = t.matmul(
                            pO[half][:],
                            yfm[tt % 2][:, fc * 128 : (fc + 1) * 128],
                            wot[:, fc * 1024 + half * 512 : fc * 1024 + (half + 1) * 512],
                            start=(fc == 0),
                            stop=(fc == 1),
                        )
                sPE.inc(li, 1, m("b2_wo", tt))

        def emit_scalar(s):
            s = CEng(s, sAC)
            for i in range(NB):
                sPE.wait(s, f"wT{i}")
                nch = nchunks(i)
                dst = WBLK[i][1]
                if WBLK[i][2] == "g":
                    src = pT[i % 2][:, 0 : nch * 128].rearrange("p (c n) -> p c n", c=nch)[:, :, 0:8]
                else:
                    src = pT[i % 2][:, 0 : nch * 128].rearrange("p (c n) -> p c n", c=nch)
                sAC.inc(s.activation(dst, src, AF.Copy), 1, f"wev{i}")
            if TRUNC < 1:
                return
            sPL.wait(s, "setup_const")
            for tt in range(NT):
                sDV.wait(s, m("a_var", tt))
                if tt >= 2 and AXL >= 2:
                    sPL.wait(s, m("a_xn", tt - 2))
                s.activation(mvb[tt % 2][:, 4:5], mvb[tt % 2][:, 2:3], AF.Ln, bias=eps5[:, 0:1])
                sAC.inc(s.activation(mvb[tt % 2][:, 1:2], mvb[tt % 2][:, 4:5], AF.Exp, scale=-0.5), 1, m("a_rstd", tt))
                if AXL >= 2:
                    sPE.wait(s, m("a_xnT", tt))
                    sAC.inc(s.activation(xnfm_3d(tt), pT[tt % 2][:, 0:1024].rearrange("p (c n) -> p c n", c=8), AF.Copy), 1, m("a_xnev", tt))
                if AXL >= 4:
                    sPE.wait(s, m("a_kv", tt))
                    if tt >= 2:
                        sDV.wait(s, m("a_p", tt - 2))
                        if AXL >= 5:
                            sDV.wait(s, m("a_ktm", tt - 2))
                    s.activation(eA[tt % 2][:], pA[tt % 2][:, 0:256], AF.Relu)
                    s.activation(eB[tt % 2][:], pA[tt % 2][:, 0:256], AF.Relu, scale=-1.0)
                    sAC.inc(s.activation(eB[tt % 2][:], eB[tt % 2][:], AF.Exp, scale=-1.0), 1, m("a_kelu", tt))
                    sAC.inc(s.activation(vaug_hd(tt)[:, :, 0:64], pA[tt % 2][:, 256:512].rearrange("p (h c) -> p h c", h=4), AF.Copy), 1, m("a_vev", tt))
                    sDV.wait(s, m("a_kred", tt))
                    s.activation(nrm[tt % 2][:, 4:8], nrm[tt % 2][:, 0:4], AF.Ln)
                    sAC.inc(s.activation(nrm[tt % 2][:, 4:8], nrm[tt % 2][:, 4:8], AF.Exp, scale=0.5), 1, m("a_ksq", tt))
                if AXL >= 5:
                    sPE.wait(s, m("a_kT", tt))
                    sAC.inc(s.activation(kfm_3d(tt), pT[(tt + 1) % 2][0:64, 0:512].rearrange("p (c n) -> p c n", c=4), AF.Copy), 1, m("a_kfev", tt))
            sAC.mark_only("endA_ac")
            if TRUNC < 3:
                return
            for tt in range(NT):
                sDV.wait(s, m("b1_gmax", tt))
                sAC.inc(s.activation(gsc[tt % 2][:], pM[tt % 2][:, 0:8], AF.Exp, bias=mvb[tt % 2][:, 5:6]), 1, m("b1_ge", tt))
                sPE.wait(s, m("b1_uT", tt))
                if tt >= 2:
                    sPE.wait(s, m("b1_out", tt - 2))
                sAC.inc(s.activation(uT[tt % 2][:], pT[tt % 2][:, 0:128], AF.Copy), 1, m("b1_uTe", tt))
            sAC.mark_only("endB1_ac")
            if TRUNC < 4:
                return
            for tt in range(NT):
                sPE.wait(s, m("b2_q", tt))
                if tt >= 2:
                    sDV.wait(s, m("b2_p", tt - 2))
                    sDV.wait(s, m("b2_qtm", tt - 2))
                s.activation(eA[tt % 2][:], pM[tt % 2][:, 0:256], AF.Relu)
                s.activation(eB[tt % 2][:], pM[tt % 2][:, 0:256], AF.Relu, scale=-1.0)
                sAC.inc(s.activation(eB[tt % 2][:], eB[tt % 2][:], AF.Exp, scale=-1.0), 1, m("b2_qelu", tt))
                sDV.wait(s, m("b2_qred", tt))
                s.activation(nrm[tt % 2][:, 4:8], nrm[tt % 2][:, 0:4], AF.Ln)
                sAC.inc(s.activation(nrm[tt % 2][:, 4:8], nrm[tt % 2][:, 4:8], AF.Exp, scale=0.5), 1, m("b2_qsq", tt))
                sPE.wait(s, m("b2_qT", tt))
                if tt >= 2:
                    sPE.wait(s, m("b2_num", tt - 2))
                sAC.inc(s.activation(
                    qfm[tt % 2][0:64, :],
                    pT[tt % 2][0:64, 256:768],
                    AF.Copy,
                ), 1, m("b2_qfev", tt))
                if BXL < 4:
                    continue
                sPE.wait(s, m("b2_num", tt))
                if tt >= 2:
                    sDV.wait(s, m("b2_rc", tt - 2))
                for h in range(H):
                    li = s.activation(dss[tt % 2][:, h : h + 1], pO[0][:, h * 65 + 64 : h * 65 + 65], AF.Identity, bias=eps6[:, 0:1])
                sAC.inc(li, 1, m("b2_de", tt))
                if BXL < 5:
                    continue
                sDV.wait(s, m("b2_rc", tt))
                if tt >= 2:
                    sPE.wait(s, m("b2_yT", tt - 2))
                for h in range(H):
                    li = s.activation(
                        ytm[tt % 2][:, h * 64 : (h + 1) * 64],
                        pO[0][:, h * 65 : h * 65 + 64],
                        AF.Identity,
                        scale=dss[tt % 2][:, 4 + h : 5 + h],
                    )
                sAC.inc(li, 1, m("b2_y", tt))
                sPE.wait(s, m("b2_yT", tt))
                if tt >= 2:
                    sPE.wait(s, m("b2_wo", tt - 2))
                sAC.inc(s.activation(
                    yfm[tt % 2][:].rearrange("p (c n) -> p c n", c=2),
                    pT[tt % 2][:, 512:768].rearrange("p (c n) -> p c n", c=2),
                    AF.Copy,
                ), 1, m("b2_yfev", tt))
            sAC.mark_only("endB2_ac")

        def emit_vector(v):
            v = CEng(v, sDV)
            sPL.wait(v, "setup_const")
            dC.wait(v, "consts")
            if TRUNC < 1:
                return
            for tt in range(NT):
                dX[tt % 2].wait(v, m("ax", tt))
                v.bn_stats(st12[tt % 2][:, 0:6], xbuf[tt % 2][:, 0:512])
                sDV.inc(v.bn_stats(st12[tt % 2][:, 6:12], xbuf[tt % 2][:, 512:1024]), 1, m("a_bn", tt))
                sDV.wait(v, m("a_bn", tt))
                if tt >= 2:
                    sPL.wait(v, m("a_xn", tt - 2))
                    sAC.wait(v, m("a_rstd", tt - 2))
                v.bn_aggr(mvb[tt % 2][:, 4:6], st12[tt % 2][:, 0:6])
                sDV.inc(v.bn_aggr(mvb[tt % 2][:, 6:8], st12[tt % 2][:, 6:12]), 1, m("a_ag", tt))
                sDV.wait(v, m("a_ag", tt))
                v.tensor_tensor(mvb[tt % 2][:, 0:1], mvb[tt % 2][:, 4:5], mvb[tt % 2][:, 6:7], ALU.add)
                v.tensor_tensor(mvb[tt % 2][:, 2:3], mvb[tt % 2][:, 5:6], mvb[tt % 2][:, 7:8], ALU.add)
                sDV.inc(v.tensor_tensor(mvb[tt % 2][:, 3:4], mvb[tt % 2][:, 4:5], mvb[tt % 2][:, 6:7], ALU.subtract), 1, m("a_cmb", tt))
                sDV.wait(v, m("a_cmb", tt))
                v.tensor_scalar_mul(mvb[tt % 2][:, 0:1], mvb[tt % 2][:, 0:1], 0.5)
                v.tensor_scalar_mul(mvb[tt % 2][:, 2:3], mvb[tt % 2][:, 2:3], 0.5)
                sDV.inc(v.tensor_tensor(mvb[tt % 2][:, 3:4], mvb[tt % 2][:, 3:4], mvb[tt % 2][:, 3:4], ALU.mult), 1, m("a_cmb2", tt))
                sDV.wait(v, m("a_cmb2", tt))
                v.tensor_scalar_mul(mvb[tt % 2][:, 3:4], mvb[tt % 2][:, 3:4], 0.25)
                sDV.inc(
                    v.tensor_tensor(mvb[tt % 2][:, 2:3], mvb[tt % 2][:, 2:3], mvb[tt % 2][:, 3:4], ALU.add),
                    1,
                    m("a_var", tt),
                )
                if AXL >= 4:
                    sDV.mark_only(m("a_kmin", tt))
                    sDV.mark_only(m("a_krelu", tt))
                    sAC.wait(v, m("a_kelu", tt))
                    sDV.inc(v.tensor_tensor(eA[tt % 2][:], eA[tt % 2][:], eB[tt % 2][:], ALU.add), 1, m("a_p", tt))
                    sDV.inc(v.tensor_tensor(eB[tt % 2][:], eA[tt % 2][:], eA[tt % 2][:], ALU.mult), 1, m("a_ksqr", tt))
                    sDV.wait(v, m("a_ksqr", tt))
                    if tt >= 2:
                        sAC.wait(v, m("a_ksq", tt - 2))
                    sDV.inc(
                        v.tensor_reduce(
                            nrm[tt % 2][:, 0:4].unsqueeze(2),
                            eB[tt % 2][:].rearrange("p (a b) -> p a b", a=4),
                            AX.X,
                            ALU.add,
                        ),
                        1,
                        m("a_kred", tt),
                    )
                    sAC.wait(v, m("a_ksq", tt))
                    sDV.inc(v.tensor_scalar_add(nrm[tt % 2][:, 4:8], nrm[tt % 2][:, 4:8], EPS_PU), 1, m("a_keps", tt))
                    sDV.wait(v, m("a_keps", tt))
                    sDV.inc(v.reciprocal(nrm[tt % 2][:, 4:8], nrm[tt % 2][:, 4:8]), 1, m("a_krec", tt))
                if AXL >= 5:
                    sDV.wait(v, m("a_krec", tt))
                    if tt >= 2:
                        sPE.wait(v, m("a_dS", tt - 2))
                    for h in range(H):
                        li = v.tensor_scalar_mul(
                            ktm[tt % 2][:, h * 64 : (h + 1) * 64],
                            eA[tt % 2][:, h * 64 : (h + 1) * 64],
                            nrm[tt % 2][:, 4 + h : 5 + h],
                        )
                    sDV.inc(li, 1, m("a_ktm", tt))
                if AXL >= 6:
                    sPE.wait(v, m("a_dS", tt))
                    v.nop(cycle_cnt=255)
                    v.nop(cycle_cnt=255)
                    sDV.inc(v.tensor_tensor(s_run[0:64, :], s_run[0:64, :], pM[tt % 2][0:64, 0:260], ALU.add), 1, m("a_Sa", tt))
                    sDV.wait(v, m("a_Sa", tt))
                    sDV.inc(v.tensor_copy(pref_v(tt + 1), s_run[0:64, :]), 1, m("a_S", tt))
            sDV.mark_only("endA_dv")
            if TRUNC >= 3:
                _b1only = True
            else:
                _b1only = False
            for tt in (range(NT) if _b1only else []):
                sPE.wait(v, m("b1_gu", tt))
                if tt >= 2:
                    sAC.wait(v, m("b1_ge", tt - 2))
                sDV.inc(
                    v.reduce_max(mvb[tt % 2][:, 5:6], pM[tt % 2][:, 0:8], AX.X, negate=True),
                    1,
                    m("b1_gmax", tt),
                )
                sAC.wait(v, m("b1_ge", tt))
                sDV.inc(v.reduce_sum(mvb[tt % 2][:, 6:7], gsc[tt % 2][:], AX.X), 1, m("b1_gsum", tt))
                sDV.wait(v, m("b1_gsum", tt))
                sDV.inc(v.reciprocal(mvb[tt % 2][:, 6:7], mvb[tt % 2][:, 6:7]), 1, m("b1_grec", tt))
                sDV.wait(v, m("b1_grec", tt))
                sDV.inc(
                    v.tensor_scalar(gsc[tt % 2][:], gsc[tt % 2][:], mvb[tt % 2][:, 6:7], 1.0 / 16.0, ALU.mult, ALU.mult),
                    1,
                    m("b1_gn", tt),
                )
                sDV.wait(v, m("b1_gn", tt))
                if tt >= 2:
                    sPE.wait(v, m("b1_uT", tt - 2))
                for e in range(8):
                    li = v.tensor_scalar_mul(
                        ubf[tt % 2][:, e * 16 : (e + 1) * 16],
                        pM[tt % 2][:, 8 + e * 16 : 8 + (e + 1) * 16],
                        gsc[tt % 2][:, e : e + 1],
                    )
                sDV.inc(li, 1, m("b1_u", tt))
                sPE.wait(v, m("b1_out", tt))
                dX[tt % 2].wait(v, m("b1x", tt))
                if tt >= 2:
                    dPS[tt % 2].wait(v, m("b1st", tt - 2))
                oBv = pO if tt % 2 == 0 else pA
                v.tensor_tensor(pbuf[tt % 2][:, 0:512], oBv[0][:], xbuf[tt % 2][:, 0:512], ALU.add)
                sDV.inc(
                    v.tensor_tensor(pbuf[tt % 2][:, 512:1024], oBv[1][:], xbuf[tt % 2][:, 512:1024], ALU.add),
                    1,
                    m("b1_part", tt),
                )
            sDV.mark_only("endB1_dv")
            if TRUNC < 2:
                return
            dS.wait(v, "gath")
            sDV.inc(v.tensor_scalar_mul(s_prev[0:64, :], gath[0:64, :], rolet[0:64, 0:1]), 1, "sprev")
            sDV.mark_only("sprev_m")
            if TRUNC < 4:
                return
            for tt in range(NT):
                sDV.wait(v, "sprev")
                if BXL >= 4:
                    if tt >= 2:
                        sPE.wait(v, m("b2_num", tt - 2))
                    sDV.inc(v.tensor_tensor(s_pk[tt % 2][0:64, :], pref_v(tt), s_prev[0:64, :], ALU.add), 1, m("b2_spk", tt))
                sDV.mark_only(m("b2_qmin", tt))
                sDV.mark_only(m("b2_qrelu", tt))
                sAC.wait(v, m("b2_qelu", tt))
                sDV.inc(v.tensor_tensor(eA[tt % 2][:], eA[tt % 2][:], eB[tt % 2][:], ALU.add), 1, m("b2_p", tt))
                sDV.inc(v.tensor_tensor(eB[tt % 2][:], eA[tt % 2][:], eA[tt % 2][:], ALU.mult), 1, m("b2_qsqr", tt))
                sDV.wait(v, m("b2_qsqr", tt))
                if tt >= 2:
                    sAC.wait(v, m("b2_qsq", tt - 2))
                sDV.inc(
                    v.tensor_reduce(
                        nrm[tt % 2][:, 0:4].unsqueeze(2),
                        eB[tt % 2][:].rearrange("p (a b) -> p a b", a=4),
                        AX.X,
                        ALU.add,
                    ),
                    1,
                    m("b2_qred", tt),
                )
                sAC.wait(v, m("b2_qsq", tt))
                sDV.inc(v.tensor_scalar_add(nrm[tt % 2][:, 4:8], nrm[tt % 2][:, 4:8], EPS_PU), 1, m("b2_qeps", tt))
                sDV.wait(v, m("b2_qeps", tt))
                sDV.inc(v.reciprocal(nrm[tt % 2][:, 4:8], nrm[tt % 2][:, 4:8]), 1, m("b2_qrec", tt))
                sDV.wait(v, m("b2_qrec", tt))
                if tt >= 2:
                    sPE.wait(v, m("b2_qT", tt - 2))
                for h in range(H):
                    li = v.tensor_scalar_mul(
                        ktm[tt % 2][:, h * 64 : (h + 1) * 64],
                        eA[tt % 2][:, h * 64 : (h + 1) * 64],
                        nrm[tt % 2][:, 4 + h : 5 + h],
                    )
                sDV.inc(li, 1, m("b2_qtm", tt))
                if BXL < 3:
                    continue
                sPE.wait(v, m("b2_AT", tt))
                if tt >= 2 and BXL >= 4:
                    sPE.wait(v, m("b2_num", tt - 2))
                for h in range(H):
                    li = v.tensor_tensor(
                        ams[tt % 2][:, h * 128 : (h + 1) * 128],
                        pA[tt % 2][:, h * 128 : (h + 1) * 128],
                        mskt[:],
                        ALU.mult,
                    )
                sDV.inc(li, 1, m("b2_am", tt))
                if BXL < 5:
                    continue
                sAC.wait(v, m("b2_de", tt))
                sDV.inc(v.reciprocal(dss[tt % 2][:, 4:8], dss[tt % 2][:, 0:4]), 1, m("b2_rc", tt))
                if BXL < 6:
                    continue
                sPE.wait(v, m("b2_wo", tt))
                dPL[tt % 2].wait(v, m("b2pl", tt))
                if tt >= 2:
                    dO[tt % 2].wait(v, m("ost", tt - 2))
                else:
                    dPS[tt % 2].wait(v, m("b1st", NT - 2 + tt))
                v.tensor_tensor(pbuf[tt % 2][:, 0:512], pO[0][:], xbuf[tt % 2][:, 0:512], ALU.add)
                sDV.inc(
                    v.tensor_tensor(pbuf[tt % 2][:, 512:1024], pO[1][:], xbuf[tt % 2][:, 512:1024], ALU.add),
                    1,
                    m("b2_out", tt),
                )
            sDV.mark_only("endB2_dv")

        _COUNTING[0] = True
        for f in (emit_sync, emit_gpsimd, emit_tensor, emit_scalar, emit_vector):
            f(_FakeEng())
        _COUNTING[0] = False
        for c in (sPE, sDV, sAC, sPL, dC, dS, *dX, *dW, *dPS, *dPL, *dO):
            c.n = 0
        block.sync(emit_sync)
        block.gpsimd(emit_gpsimd)
        block.tensor(emit_tensor)
        block.scalar(emit_scalar)
        block.vector(emit_vector)

    return nc


_NC_CACHE = {}


def kernel(
    L0_attn_x,
    L0_attn_Wq,
    L0_attn_Wk,
    L0_attn_Wv,
    L0_attn_Wo,
    L0_ffn_weight,
    L0_ffn_gate_w,
    L0_ffn_lora_U,
    L0_ffn_lora_V,
):
    x = np.ascontiguousarray(np.asarray(L0_attn_x, np.float32))
    W = {
        "Wq": np.ascontiguousarray(np.asarray(L0_attn_Wq, np.float32)),
        "Wk": np.ascontiguousarray(np.asarray(L0_attn_Wk, np.float32)),
        "Wv": np.ascontiguousarray(np.asarray(L0_attn_Wv, np.float32)),
        "Wo": np.ascontiguousarray(np.asarray(L0_attn_Wo, np.float32)),
        "Wff": np.ascontiguousarray(np.asarray(L0_ffn_weight, np.float32)),
        "gw": np.ascontiguousarray(np.asarray(L0_ffn_gate_w, np.float32)),
        "U2": np.ascontiguousarray(np.asarray(L0_ffn_lora_U, np.float32).reshape(128, 1024)),
        "loraV": np.ascontiguousarray(np.asarray(L0_ffn_lora_V, np.float32)),
    }
    idn = np.eye(128, dtype=ml_dtypes.bfloat16)
    mask = np.triu(np.ones((128, 128), np.float32))
    if "nc" not in _NC_CACHE:
        _NC_CACHE["nc"] = build_nc()
    nc = _NC_CACHE["nc"]
    in_maps = []
    for c in range(8):
        b, half = c // 2, c % 2
        in_maps.append(
            {
                "x": np.ascontiguousarray(x[b, half * 4096 : (half + 1) * 4096, :]),
                "Wq": W["Wq"], "Wk": W["Wk"], "Wv": W["Wv"], "Wo": W["Wo"],
                "Wff": W["Wff"], "gw": W["gw"], "U2": W["U2"], "loraV": W["loraV"],
                "idn": idn, "maskUT": mask,
                "role": np.full((128, 1), float(half), np.float32),
            }
        )
    res = run_bass_kernel_spmd(nc, in_maps, core_ids=list(range(8)))
    out = np.empty((4, 8192, 1024), np.float32)
    for c in range(8):
        b, half = c // 2, c % 2
        out[b, half * 4096 : (half + 1) * 4096, :] = res.results[c]["out"]
    if KDEBUG:
        kernel._last = res
    return out
